# version 01 (final)
# baseline (speedup 1.0000x reference)
import numpy as np
import concourse.bass as bass
import concourse.mybir as mybir
from concourse.bass_utils import run_bass_kernel_spmd

F32 = mybir.dt.float32
F32R = mybir.dt.float32r
BF16 = mybir.dt.bfloat16
AF = mybir.ActivationFunctionType
OP = mybir.AluOpType
AX = mybir.AxisListType.X

T = 2048
CH = 8
D = 1024
TE = 2052
TEP = 2176
KTB = TEP // 128
NTB = 4

_STAGE = 99
_TAPS = ()
_ACUT = 99
_FCUT = 99



def _host_consts():
    c = {}
    ident = np.eye(128, dtype=np.float32)
    c["identf"] = ident
    c["identb"] = ident.copy()
    c["identb64"] = np.tile(np.eye(64, dtype=np.float32), (2, 1))
    c["identb64f"] = c["identb64"].copy()
    c["onesb"] = np.ones((128, 128), np.float32)
    c["onesr"] = np.ones((128, 128), np.float32)
    c["onescol"] = np.ones((128, 1), np.float32)
    c["trilb"] = np.triu(np.ones((128, 128), np.float32)).copy()
    c["selcol8"] = np.stack([np.repeat(np.eye(8, dtype=np.float32)[:, e:e + 1], 128, 1)
                             for e in range(8)], axis=1)
    c["selcol2"] = np.stack([np.repeat(np.eye(2, dtype=np.float32)[:, r:r + 1], 128, 1)
                             for r in range(2)], axis=1)
    bd2 = np.zeros((128, 2), np.float32)
    bd2[:64, 0] = 1.0
    bd2[64:, 1] = 1.0
    c["bd2"] = bd2
    c["bd2t"] = bd2.T.copy()
    swp = np.zeros((128, 128), np.float32)
    for m in range(128):
        blk, d_ = m // 64, m % 64
        swp[blk * 64 + ((d_ + 32) % 64), m] = 1.0
    c["swapperm"] = swp
    dh = 64
    inv = 1.0 / (10000.0 ** (np.arange(0, dh, 2, dtype=np.float64) / dh))
    pos = np.arange(TEP, dtype=np.float64)
    f = pos[None, :] * inv[:, None]
    cosf, sinf = np.cos(f), np.sin(f)
    tc_ = np.empty((128, TEP), np.float64)
    ts_ = np.empty((128, TEP), np.float64)
    for p in range(128):
        j = p % 32
        lo = (p % 64) < 32
        tc_[p] = cosf[j]
        ts_[p] = -sinf[j] if lo else sinf[j]
    c["rotc"] = tc_.astype(np.float32)
    c["rots"] = ts_.astype(np.float32)
    bd8 = np.zeros((128, 8), np.float32)
    for p in range(128):
        bd8[p, p // 16] = 1.0
    c["bd8"] = bd8
    selx0 = np.zeros((128, 4, 8), np.float32)
    for cc in range(4):
        selx0[0, cc, 2 * cc] = 1.0
        selx0[64, cc, 2 * cc + 1] = 1.0
    c["selx0"] = selx0
    e8c = np.zeros((8, 4, 128), np.float32)
    for cc in range(4):
        e8c[2 * cc, cc, :64] = 1.0
        e8c[2 * cc + 1, cc, 64:] = 1.0
    c["e8c"] = e8c
    c["constv"] = np.repeat(np.array([[0.0, 1.0, -1.0, 1e-5, 1e-6, -0.5, 0.125,
                                       9.0, 0.5, 1.0 / 64, -2.0794415416798357,
                                       -30.0]], np.float32), 128, 0)
    return c


def _host_prep(inputs):
    g = lambda k: np.ascontiguousarray(np.asarray(inputs[k], np.float32))
    d = {}
    d["wqkv"] = g("L0_attn_a_w_qkv")
    d["wproja"] = g("L0_attn_a_w_proj")
    d["bproja"] = g("L0_attn_a_b_proj").reshape(CH, 128).T.copy()
    cqt = g("L0_attn_b_c_q_w")[:512].T
    qperm = np.arange(512)
    for h in range(8):
        pos = 128 * (2 * (h // 4) + (h % 2)) + 64 * ((h // 2) % 2)
        qperm[pos:pos + 64] = np.arange(64 * h, 64 * h + 64)
    d["cqwT"] = np.ascontiguousarray(cqt[:, qperm])
    d["ckwT"] = g("L0_attn_b_c_k_w").T.copy()
    d["cvwT"] = g("L0_attn_b_c_v_w").T.copy()
    d["ssmwT"] = g("L0_attn_b_ssm_in_w").T.copy()
    pbt = g("L0_attn_b_proj_w").T.copy()
    h_at = {(c, 0): [0, 1, 4, 5][c] for c in range(4)}
    h_at.update({(c, 1): [2, 3, 6, 7][c] for c in range(4)})
    perm = np.arange(1024)
    for c in range(4):
        for b in range(2):
            for dd in range(64):
                perm[128 * c + 64 * b + dd] = 64 * h_at[(c, b)] + dd
    d["projbT"] = np.ascontiguousarray(pbt[perm])
    d["gatewT"] = g("L0_ffn_gate_w").T.copy()
    d["expwT"] = g("L0_ffn_expert_w").transpose(0, 2, 1).copy()
    d["meta"] = g("L0_attn_b_meta_tokens").reshape(4, 1024)
    d["gainrow"] = g("L0_attn_b_q_gain").reshape(1, 8)
    d["ascale"] = g("L0_attn_b_attn_scale").reshape(1, 1)
    d["sscale"] = g("L0_attn_b_ssm_scale").reshape(1, 1)
    d["logA"] = g("L0_attn_b_ssm_logA").reshape(128, 1)
    d["dtb"] = g("L0_attn_b_ssm_dtb").reshape(128, 1)
    dtw = g("L0_attn_b_ssm_dtw")
    bw = g("L0_attn_b_ssm_Bw")
    cw = g("L0_attn_b_ssm_Cw")
    for nm, w in (("zdt", dtw), ("zbx", bw), ("zcc", cw)):
        z = np.zeros((128, 4, 128), np.float32)
        for h in range(8):
            z[64 * (h % 2):64 * (h % 2) + 64, h // 2, 16 * h:16 * h + 16] = w[h].T
        d[nm] = z
    ow = g("L0_attn_b_ssm_Ow")
    bdow = np.zeros((128, 4, 128), np.float32)
    for h in range(8):
        cp_, i = h // 2, h % 2
        bdow[32 * cp_ + 16 * i: 32 * cp_ + 16 * i + 16, cp_,
             64 * i: 64 * i + 64] = ow[h].T
    d["bdow"] = bdow
    d.update(_host_consts())
    return d


class Sems:
    _PFX = [("f1ln", "L~"), ("f2ln", "L~"), ("ln1", "L~"), ("ln3", "L~"),
            ("f1", "F~"), ("f2", "F~")]
    _ALIAS = {"otp": "xtp", "oev": "xev", "odma0": "xdma0", "odma1": "xdma1",
              "bascr": "ascr", "baavk": "aavk", "baexp": "aexp",
              "barc": "arc", "barcbc": "arcbc", "banorm": "anorm"}

    def __init__(self, nc):
        self.nc = nc
        self.h = {}
        self.c = {}
        self.base = {}

    def _key(self, name):
        for p, r in self._PFX:
            if name.startswith(p):
                return r + name[len(p):]
        return self._ALIAS.get(name, name)

    def tot(self, name):
        return self.c[self._key(name)] - self.base.get(name, 0)

    def get(self, name):
        k = self._key(name)
        if k not in self.h:
            self.h[k] = self.nc.alloc_semaphore(k)
            self.c[k] = 0
        if name not in self.base:
            self.base[name] = self.c[k]
        return self.h[k]

    def inc(self, instr, name, amt=1):
        instr.then_inc(self.get(name), amt)
        k = self._key(name)
        self.c[k] += amt
        return self.c[k] - self.base[name]

    def wait(self, eng, name, v):
        eng.wait_ge(self.get(name), v + self.base[name])


def build(stage=99, taps=(), acut=None, fcut=None):
    skip_p7b = (stage == 70)
    if skip_p7b:
        stage = 7
    acut = _ACUT if acut is None else acut
    fcut = _FCUT if fcut is None else fcut
    nc = bass.Bass("TRN2", target_bir_lowering=False, detect_race_conditions=False)
    S = Sems(nc)

    x_d = nc.dram_tensor("x", [T, D], F32, kind="ExternalInput")
    out_d = nc.dram_tensor("out", [T, D], F32, kind="ExternalOutput")
    dr = {}
    for nm, shp in [
        ("wqkv", [D, 3072]), ("wproja", [D, D]), ("bproja", [128, 8]),
        ("cqwT", [D, 512]), ("ckwT", [D, 256]), ("cvwT", [D, 256]),
        ("ssmwT", [D, 512]), ("projbT", [D, D]), ("gatewT", [D, 8]),
        ("expwT", [8, D, D]), ("meta", [4, D]), ("gainrow", [1, 8]),
        ("ascale", [1, 1]), ("sscale", [1, 1]), ("logA", [128, 1]),
        ("dtb", [128, 1]), ("zdt", [128, 4, 128]), ("zbx", [128, 4, 128]),
        ("zcc", [128, 4, 128]), ("bdow", [128, 4, 128]), ("identf", [128, 128]),
        ("identb64", [128, 64]), ("identb64f", [128, 64]),
        ("identb", [128, 128]), ("onesb", [128, 128]), ("onesr", [128, 128]),
        ("onescol", [128, 1]), ("trilb", [128, 128]), ("selcol8", [8, 8, 128]),
        ("selcol2", [2, 2, 128]), ("bd2", [128, 2]), ("bd2t", [2, 128]),
        ("swapperm", [128, 128]), ("rotc", [128, TEP]), ("rots", [128, TEP]),
        ("bd8", [128, 8]), ("selx0", [128, 4, 8]), ("e8c", [8, 4, 128]),
        ("constv", [128, 12]),
    ]:
        dr[nm] = nc.dram_tensor(nm, shp, F32, kind="ExternalInput")

    AL = [16384]

    def sb(name, shape, dtype, align=64):
        per = int(np.prod(shape[1:])) * mybir.dt.size(dtype)
        AL[0] = (AL[0] + align - 1) // align * align
        t = nc.alloc_sbuf_tensor_at(name, shape, dtype, offset=AL[0])
        AL[0] += per
        return t

    SBT = {}

    def sbp(name, shape, dtype):
        t = sb(name, shape, dtype)
        SBT[name] = t
        return t

    X = sbp("X", [128, CH, T], F32)
    XN = sbp("XN", [128, CH, T], BF16)
    AO = sbp("AO", [128, CH, T], BF16)
    identf = sb("identf_s", [128, 128], F32)
    identb = sb("identb_s", [128, 128], BF16)
    identb64 = sb("identb64_s", [128, 64], BF16)
    identb64f = sb("identb64f_s", [128, 64], F32)
    onesb = sb("onesb_s", [128, 128], BF16)
    onesr_f = sb("onesr_s", [128, 128], F32)
    onescol = sb("onescol_s", [128, 1], BF16)
    trilb = sb("trilb_s", [128, 128], BF16)
    selcol8 = sb("selcol8_s", [8, 8, 128], BF16)
    selcol2 = sb("selcol2_s", [2, 2, 128], F32)
    bd2 = sb("bd2_s", [128, 2], BF16)
    bd2t_f = sb("bd2t_s", [2, 128], F32)
    swapperm_f = sb("swapperm_s", [128, 128], F32)
    bd8 = sb("bd8_s", [128, 8], BF16)
    selx0 = sb("selx0_s", [128, 4, 8], BF16)
    e8c_f = sb("e8c_s", [8, 4, 128], F32)
    bdow = sb("bdow_s", [128, 4, 128], BF16)
    zdt = sb("zdt_s", [128, 4, 128], BF16)
    zbx = sb("zbx_s", [128, 4, 128], BF16)
    zcc = sb("zcc_s", [128, 4, 128], BF16)
    bproja = sb("bproja_s", [128, 8], F32)
    logA = sb("logA_s", [128, 1], F32)
    expA = sb("expA_s", [128, 1], F32)
    negA = sb("negA_s", [128, 1], F32)
    dtb = sb("dtb_s", [128, 1], F32)
    gainrow = sb("gainrow_s", [1, 8], F32)
    ascale = sb("ascale_s", [1, 1], F32)
    sscale = sb("sscale_s", [1, 1], F32)
    ascale_b = sb("ascale_b", [128, 1], F32)
    sscale_b = sb("sscale_b", [128, 1], F32)
    gs_b = sb("gs_b", [128, 8], F32)
    gs_b2 = sb("gs_b2", [128, 8], F32)
    bb_b = sb("bb_b", [128, 8], F32)
    constv = sb("constv_s", [128, 12], F32)
    for _i, _v in enumerate([0.0, 1.0, -1.0, 1e-5, 1e-6, -0.5, 0.125, 9.0, 0.5,
                             1.0 / 64, -2.0794415416798357, -30.0]):
        nc.const_aps.register(F32, _v, constv[:, _i:_i + 1])

    onesr = onesr_f
    bd2t = bd2t_f
    swapperm = swapperm_f
    e8c = e8c_f

    dbgnm = sb("a_dbgnm", [2, 8, 16], F32)
    SBT["adbgnm"] = dbgnm
    junkA = sb("junkA", [128, 64], F32)
    junkB = sb("junkB", [128, 64], F32)

    PH = AL[0]

    def arena():
        AL[0] = PH

    pb = [nc.alloc_psum_tensor("pb%d" % i, [128, 512], F32) for i in range(8)]

    ksr = nc._kernel_sem_range
    nmono = len(nc._monotonic_sems)
    base = ksr.start + 2 + (1 if nc._bir_kernel_barrier_sem is not None else 0) + nmono
    with nc.Block() as blk:
        @blk.gpsimd
        def _(g_):
            g_.dma_reset(range(base, ksr.stop))
            g_.sem_clear(range(base, ksr.stop))

    NCONST_SYNC = 14
    NCONST_GP = 13

    arena()
    xt = sb("xt", [128, 2, 1024], F32)

    with nc.Block() as blk:
        @blk.gpsimd
        def _(g):
            for nm, dst in [("identb", identb), ("identb64", identb64),
                            ("onesb", onesb), ("onescol", onescol),
                            ("trilb", trilb), ("selcol8", selcol8), ("bd2", bd2),
                            ("bd8", bd8), ("selx0", selx0), ("bdow", bdow)]:
                S.inc(g.dma_start(dst[...], dr[nm][...]), "wdma", 16)
            for nm, dst in [("zdt", zdt), ("zbx", zbx), ("zcc", zcc)]:
                S.inc(g.dma_start(dst[...], dr[nm][...].transpose([1, 0, 2])), "wdma", 16)

        @blk.sync
        def _(sy):
            for nm, dst in [("identf", identf), ("onesr", onesr_f), ("selcol2", selcol2),
                            ("bd2t", bd2t_f), ("swapperm", swapperm_f), ("e8c", e8c_f),
                            ("bproja", bproja), ("logA", logA), ("dtb", dtb),
                            ("identb64f", identb64f),
                            ("gainrow", gainrow), ("ascale", ascale), ("sscale", sscale),
                            ("constv", constv)]:
                S.inc(sy.dma_start(dst[...], dr[nm][...]), "cdma", 16)
            for i in range(16):
                if i >= 2:
                    S.wait(sy, "xtp", 2 * (i - 2) + 2)
                S.inc(sy.dma_start(xt[:, i % 2, :], x_d[128 * i:128 * i + 128, :]),
                      "xdma%d" % (i % 2), 16)

        @blk.tensor
        def _(t):
            S.wait(t, "cdma", 16 * NCONST_SYNC)
            for i in range(16):
                S.wait(t, "xdma%d" % (i % 2), 16 * (i // 2 + 1))
                if i >= 2:
                    S.wait(t, "xev", 2 * (i - 2) + 2)
                for c in range(CH):
                    bank = pb[(i % 2) * 2 + c // 4]
                    ins = t.transpose(bank[:, 128 * (c % 4):128 * (c % 4) + 128],
                                      xt[:, i % 2, 128 * c:128 * c + 128], identf[...])
                    if c % 4 == 3:
                        S.inc(ins, "xtp", 1)

        @blk.vector
        def _(v):
            v.memset(junkA[...], 0.0)
            for i in range(16):
                for b in range(2):
                    S.wait(v, "xtp", 2 * i + b + 1)
                    bank = pb[(i % 2) * 2 + b]
                    ins = v.tensor_copy(
                        X[:, 4 * b:4 * b + 4, 128 * i:128 * i + 128],
                        bank[...].rearrange("p (c q) -> p c q", c=4))
                    S.inc(ins, "xev", 1)

    with nc.Block() as blk:
        @blk.scalar
        def _(s):
            S.wait(s, "cdma", 16 * NCONST_SYNC)
            S.wait(s, "wdma", 16 * NCONST_GP)
            s.activation(expA[...], logA[...], AF.Exp)

    with nc.Block() as blk:
        @blk.vector
        def _(v):
            v.tensor_scalar_mul(negA[...], expA[...], -1.0)

        @blk.tensor
        def _(t):
            t.matmul(pb[7][:, 0:1], onesr_f[0:1, :], ascale[...], start=True, stop=True)
            t.matmul(pb[7][:, 1:2], onesr_f[0:1, :], sscale[...], start=True, stop=True)
            t.matmul(pb[7][:, 2:10], onesr_f[0:1, :], gainrow[...], start=True, stop=True)

    with nc.Block() as blk:
        @blk.vector
        def _(v):
            v.tensor_copy(ascale_b[...], pb[7][:, 0:1])
            v.tensor_copy(sscale_b[...], pb[7][:, 1:2])
            v.tensor_scalar_mul(gs_b[...], pb[7][:, 2:10], 0.125)

    with nc.Block() as blk:
        @blk.scalar
        def _(s):
            s.activation(gs_b2[...], pb[7][:, 2:10], AF.Abs, scale=9.0)

    with nc.Block() as blk:
        @blk.vector
        def _(v):
            v.tensor_scalar_mul(bb_b[...], gs_b2[...], -1.0)

    def emit_ln(tag, dst_tile, negmean=None, reset=True):
        if reset:
            arena()
        sq = sb(tag + "_sq", [128, 2, 512], BF16)
        tmp = sb(tag + "_tmp", [128, 512], F32)
        srow = sb(tag + "_srow", [1, 2, 512], F32)
        rowsA = sb(tag + "_rows", [1, 4, 512], F32)

        with nc.Block() as blk:
            @blk.vector
            def _(v):
                for c in range(CH):
                    S.inc(v.tensor_copy(dst_tile(c, None), X[:, c, :]), tag + "cp", 1)
                for tb in range(NTB):
                    for c in range(CH):
                        idx = tb * CH + c
                        if idx >= 2:
                            S.wait(v, tag + "sqf", idx - 1)
                        S.inc(v.tensor_mul(sq[:, idx % 2, :],
                                           dst_tile(c, tb), dst_tile(c, tb)), tag + "sq", 1)
                    S.wait(v, tag + "sqf", CH * (tb + 1))
                    v.tensor_scalar_mul(rowsA[:, 0, :], pb[4][0:1, :], 1.0 / 1024.0)
                    v.tensor_mul(rowsA[:, 1, :], rowsA[:, 0, :], rowsA[:, 0, :])
                    ins = v.scalar_tensor_tensor(rowsA[:, 2, :], pb[5][0:1, :], 1.0 / 1024.0,
                                                 rowsA[:, 1, :], OP.mult, OP.subtract)
                    S.inc(ins, tag + "var", 1)
                    if negmean is not None:
                        ins = v.tensor_scalar_mul(negmean[:, tb, :], rowsA[:, 0, :],
                                                  -1.0)
                        S.inc(ins, tag + "nm", 1)
                    S.wait(v, tag + "rstd", tb + 1)
                    ins = v.scalar_tensor_tensor(srow[:, 1, :], rowsA[:, 0, :], -1.0,
                                                 srow[:, 0, :], OP.mult, OP.mult)
                    S.inc(ins, tag + "brow", 1)
                    S.wait(v, tag + "bc", tb + 1)
                    for c in range(CH):
                        v.tensor_mul(tmp[...], X[:, c, 512 * tb:512 * tb + 512], pb[6][:, :])
                        ins = v.tensor_add(dst_tile(c, tb), tmp[...], pb[7][:, :])
                        if c == CH - 1:
                            S.inc(ins, tag + "ap", 1)

            @blk.scalar
            def _(s):
                for tb in range(NTB):
                    S.wait(s, tag + "var", tb + 1)
                    if tb >= 1:
                        S.wait(s, tag + "bc", tb)
                    s.activation(rowsA[:, 3, :], rowsA[:, 2, :], AF.Ln, bias=1e-5)
                    ins = s.activation(srow[:, 0, :], rowsA[:, 3, :], AF.Exp, scale=-0.5)
                    S.inc(ins, tag + "rstd", 1)

            @blk.tensor
            def _(t):
                S.wait(t, tag + "cp", CH)
                for tb in range(NTB):
                    if tb >= 1:
                        S.wait(t, tag + "var", tb)
                    for c in range(CH):
                        t.matmul(pb[4][0:1, :], onescol[...], dst_tile(c, tb),
                                 start=(c == 0), stop=(c == CH - 1))
                    for c in range(CH):
                        idx = tb * CH + c
                        S.wait(t, tag + "sq", idx + 1)
                        ins = t.matmul(pb[5][0:1, :], onescol[...], sq[:, idx % 2, :],
                                       start=(c == 0), stop=(c == CH - 1))
                        S.inc(ins, tag + "sqf", 1)
                    S.wait(t, tag + "brow", tb + 1)
                    if tb >= 1:
                        S.wait(t, tag + "ap", tb)
                    t.matmul(pb[6][:, :], onesr_f[0:1, :], srow[:, 0, :],
                             start=True, stop=True)
                    ins = t.matmul(pb[7][:, :], onesr_f[0:1, :], srow[:, 1, :],
                                   start=True, stop=True)
                    S.inc(ins, tag + "bc", 1)

    if stage >= 2:
        emit_ln("ln1", lambda c, tb: XN[:, c, :] if tb is None
                else XN[:, c, 512 * tb:512 * tb + 512])

    if stage >= 3:
        arena()
        wg = sb("a_wg", [128, 2, 8, 3, 128], BF16)
        qg = sb("a_qg", [128, 1, T], BF16)
        kg = sb("a_kg", [128, 1, T], BF16)
        vg = sb("a_vg", [128, 1, T], F32)
        vtw = sb("a_vtw", [128, 1, 16, 194], BF16)
        sqs = sb("a_sqs", [128, 512], BF16)
        stb = sb("a_stb", [128, 2, 512], BF16)
        anrow = sb("a_anrow", [128, 2, 512], F32)
        Bb = sb("a_Bb", [128, 1, 2], F32)
        recb = sb("a_recb", [128, 512], F32)

        nmax = sb("a_nmax", [2, 16], F32)
        for _n, _t in (("aqg", qg), ("akg", kg), ("avg", vg), ("avtw", vtw),
                       ("astb", stb), ("aBb", Bb), ("anmax", nmax), ("arecb", recb),
                       ("aanrow", anrow)):
            SBT[_n] = _t

        with nc.Block() as blk:
            @blk.vector
            def _(v):
                v.memset(vtw[:, :, :, 64:66], 1.0)
                v.memset(vtw[:, :, :, 66:129], 0.0)
                v.memset(nmax[...], 0.0)

        with nc.Block() as blk:
            @blk.gpsimd
            def _(g_):
                for g in range(8):
                    if g >= 2:
                        S.wait(g_, "aqkv", 12 * (g - 1))
                    for j, col0 in enumerate((128 * g, 1024 + 128 * g, 2048 + 128 * g)):
                        S.inc(g_.dma_start(
                            wg[:, g % 2, :, j, :],
                            dr["wqkv"][:, col0:col0 + 128].rearrange(
                                "(c p) j -> p c j", p=128)), "awg%d" % (g % 2), 16)

            @blk.tensor
            def _(t):
                for g in range(8):
                    par = g % 2
                    S.wait(t, "awg%d" % par, 48 * (g // 2 + 1))
                    for tb in range(NTB):
                        if 12 * g + 3 * tb > 0:
                            S.wait(t, "aqkvev", 12 * g + 3 * tb)
                        for j in range(3):
                            for c in range(CH):
                                ins = t.matmul(pb[j][:, :], wg[:, par, c, j, :],
                                               XN[:, c, 512 * tb:512 * tb + 512],
                                               start=(c == 0), stop=(c == CH - 1))
                            S.inc(ins, "aqkv", 1)
                    if acut < 1:
                        continue
                    S.wait(t, "anorm", 8 * g) if acut >= 3 else None
                    for tb in range(NTB):
                        S.wait(t, "aqkvev", 12 * g + 3 * tb + 3)
                        if 4 * g + tb > 0:
                            S.wait(t, "avtev", 4 * g + tb)
                        for kk in range(4):
                            kt = 4 * tb + kk
                            ins = t.transpose(
                                pb[6][:, 128 * kk:128 * kk + 128],
                                vg[:, 0, 128 * kt:128 * kt + 128],
                                identf[...])
                        S.inc(ins, "avtp", 1)
                    if acut < 2:
                        continue
                    for j in range(2):
                        for tb in range(NTB):
                            n = 8 * g + 4 * j + tb
                            S.wait(t, "asq", n + 1)
                            if n > 0:
                                S.wait(t, "anrmev", n)
                            ins = t.matmul(pb[7][0:2, :], bd2[...], sqs[...],
                                           start=True, stop=True)
                            S.inc(ins, "anrm", 1)
                    S.wait(t, "ab2", g + 1)
                    S.wait(t, "anrmev", 8 * g + 8)
                    for r in range(2):
                        ins = t.matmul(pb[7][:, r:r + 1], selcol2[:, r, :],
                                       nmax[:, 11:12], start=True, stop=True)
                    S.inc(ins, "ab2bc", 1)
                    if acut < 3:
                        continue
                    S.wait(t, "abb", g + 1)
                    S.wait(t, "avtev", 4 * (g + 1))
                    for h2 in range(2):
                        for qb in range(NTB):
                            un = g * 8 + h2 * 4 + qb
                            for kt in range(16):
                                if un * 16 + kt >= 2:
                                    S.wait(t, "aexp", un * 16 + kt - 1)
                                ins = t.matmul(pb[3 + kt % 2][:, :],
                                               kg[64 * h2:64 * h2 + 64, 0,
                                                  128 * kt:128 * kt + 128],
                                               qg[64 * h2:64 * h2 + 64, 0,
                                                  512 * qb:512 * qb + 512],
                                               start=True, stop=True)
                                S.inc(ins, "ascr", 1)
                                if kt >= 1:
                                    S.wait(t, "aexp", un * 16 + kt)
                                    if kt == 1 and un >= 2:
                                        S.wait(t, "anorm", un - 1)
                                    lhs = (vtw[:, 0, kt - 1, 0:65] if h2 == 0
                                           else vtw[:, 0, kt - 1, 65:193])
                                    avo = (pb[5 + un % 2][0:65, :] if h2 == 0
                                           else pb[5 + un % 2][...])
                                    ins = t.matmul(avo, lhs,
                                                   stb[:, (kt - 1) % 2, :],
                                                   start=(kt == 1), stop=False)
                                    S.inc(ins, "aavk", 1)
                            S.wait(t, "aexp", un * 16 + 16)
                            lhs = (vtw[:, 0, 15, 0:65] if h2 == 0
                                   else vtw[:, 0, 15, 65:193])
                            avo = (pb[5 + un % 2][0:65, :] if h2 == 0
                                   else pb[5 + un % 2][...])
                            ins = t.matmul(avo, lhs, stb[:, 1, :],
                                           start=False, stop=True)
                            S.inc(ins, "aavk", 1)
                            S.wait(t, "arc", un + 1)
                            if un >= 1:
                                S.wait(t, "anorm", un)
                            srp = 64 if h2 == 0 else 0
                            ins = t.matmul(pb[7][:, :], onesr_f[srp:srp + 1, 0:128],
                                           anrow[srp:srp + 1, 1, :],
                                           start=True, stop=True)
                            S.inc(ins, "arcbc", 1)

            @blk.vector
            def _(v):
                for g in range(8):
                    par = g % 2
                    for tb in range(NTB):
                        for j, buf in enumerate((qg, kg, vg)):
                            S.wait(v, "aqkv", 12 * g + 3 * tb + j + 1)
                            ins = v.tensor_copy(buf[:, 0, 512 * tb:512 * tb + 512],
                                                pb[j][:, :])
                            S.inc(ins, "aqkvev", 1)
                    if acut < 1:
                        continue
                    for tb in range(NTB):
                        S.wait(v, "avtp", 4 * g + tb + 1)
                        bview = pb[6][...].rearrange("p (b h q) -> p b h q", b=4, h=2)
                        v.tensor_copy(
                            vtw[:, 0, 4 * tb:4 * tb + 4, 0:64], bview[:, :, 0, :])
                        ins = v.tensor_copy(
                            vtw[:, 0, 4 * tb:4 * tb + 4, 129:193], bview[:, :, 1, :])
                        S.inc(ins, "avtev", 1)
                    if acut < 2:
                        continue
                    for j, buf in ((0, qg), (1, kg)):
                        for tb in range(NTB):
                            n = 8 * g + 4 * j + tb
                            if n > 0:
                                S.wait(v, "anrm", n)
                            S.inc(v.tensor_mul(sqs[...],
                                               buf[:, 0, 512 * tb:512 * tb + 512],
                                               buf[:, 0, 512 * tb:512 * tb + 512]),
                                  "asq", 1)
                            S.wait(v, "anrm", n + 1)
                            S.inc(v.reduce_max(nmax[:, 4 * j + tb:4 * j + tb + 1],
                                               pb[7][0:2, :], axis=AX), "anrmev", 1)
                    if acut < 2:
                        continue
                    v.tensor_copy(junkB[...], junkA[...])
                    v.tensor_copy(junkB[...], junkA[...])
                    v.reduce_max(nmax[:, 8:9], nmax[:, 0:4], axis=AX)
                    v.reduce_max(nmax[:, 9:10], nmax[:, 4:8], axis=AX)
                    v.tensor_copy(junkB[...], junkA[...])
                    v.tensor_copy(junkB[...], junkA[...])
                    S.inc(v.tensor_mul(nmax[:, 12:13], nmax[:, 8:9], nmax[:, 9:10]),
                          "ab2pre", 1)
                    S.wait(v, "ab2s", g + 1)
                    S.inc(v.tensor_scalar_mul(nmax[:, 11:12], nmax[:, 13:14], -1.0),
                          "ab2", 1)
                    S.wait(v, "ab2bc", g + 1)
                    v.tensor_copy(dbgnm[:, g, :], nmax[...])
                    S.inc(v.tensor_copy(Bb[:, 0, :], pb[7][:, 0:2]), "abb", 1)
                    if acut < 3:
                        continue
                    for h2 in range(2):
                        for qb in range(NTB):
                            un = g * 8 + h2 * 4 + qb
                            S.wait(v, "arcbc", un + 1)
                            av = pb[5 + un % 2]
                            src = av[0:64, :] if h2 == 0 else av[64:128, :]
                            dst = AO[64 * h2:64 * h2 + 64, g, 512 * qb:512 * qb + 512]
                            v.tensor_copy(recb[64 * h2:64 * h2 + 64, :],
                                          pb[7][64 * h2:64 * h2 + 64, :])
                            ins = v.tensor_mul(
                                dst, src, recb[64 * h2:64 * h2 + 64, :])
                            S.inc(ins, "anorm", 1)

            @blk.scalar
            def _(s):
                if acut < 2:
                    return
                for g in range(8):
                    par = g % 2
                    S.wait(s, "ab2pre", g + 1)
                    s.activation(nmax[:, 10:11], nmax[:, 12:13], AF.Ln)
                    S.inc(s.activation(nmax[:, 13:14], nmax[:, 10:11], AF.Exp,
                                       scale=0.5, bias=-2.0794415416798357), "ab2s", 1)
                    if acut < 3:
                        continue
                    for h2 in range(2):
                        for qb in range(NTB):
                            un = g * 8 + h2 * 4 + qb
                            for kt in range(16):
                                S.wait(s, "ascr", un * 16 + kt + 1)
                                if un * 16 + kt >= 2:
                                    S.wait(s, "aavk", un * 16 + kt - 1)
                                ins = s.activation(stb[:, kt % 2, :],
                                                   pb[3 + kt % 2][:, :], AF.Exp,
                                                   bias=Bb[:, 0, h2:h2 + 1],
                                                   scale=constv[:, 6:7])
                                S.inc(ins, "aexp", 1)
                            S.wait(s, "aavk", 16 * (un + 1))
                            if un >= 1:
                                S.wait(s, "arcbc", un)
                            srp = 64 if h2 == 0 else 0
                            av = pb[5 + un % 2]
                            s.activation(anrow[srp:srp + 1, 0, :],
                                         av[srp:srp + 1, :], AF.Ln)
                            S.inc(s.activation(anrow[srp:srp + 1, 1, :],
                                               anrow[srp:srp + 1, 0, :], AF.Exp,
                                               scale=-1.0), "arc", 1)

    if stage >= 4:
        arena()
        wpa = sb("pa_w", [128, 2, 8, 128], BF16)
        with nc.Block() as blk:
            @blk.gpsimd
            def _(g_):
                for i in range(CH):
                    if i >= 2:
                        S.wait(g_, "pamm", 4 * (i - 1))
                    S.inc(g_.dma_start(
                        wpa[:, i % 2, :, :],
                        dr["wproja"][:, 128 * i:128 * i + 128].rearrange(
                            "(c p) j -> p c j", p=128)), "paw%d" % (i % 2), 16)

            @blk.tensor
            def _(t):
                for i in range(CH):
                    S.wait(t, "paw%d" % (i % 2), 16 * (i // 2 + 1))
                    for tb in range(NTB):
                        un = i * NTB + tb
                        if un >= 2:
                            S.wait(t, "paev", un - 1)
                        for j in range(CH):
                            ins = t.matmul(pb[un % 2][:, :], wpa[:, i % 2, j, :],
                                           AO[:, j, 512 * tb:512 * tb + 512],
                                           start=(j == 0), stop=(j == CH - 1))
                        S.inc(ins, "pamm", 1)

            @blk.vector
            def _(v):
                for i in range(CH):
                    for tb in range(NTB):
                        un = i * NTB + tb
                        S.wait(v, "pamm", un + 1)
                        dst = X[:, i, 512 * tb:512 * tb + 512]
                        ins = v.scalar_tensor_tensor(dst, pb[un % 2][:, :],
                                                     bproja[:, i:i + 1], dst,
                                                     OP.add, OP.add)
                        S.inc(ins, "paev", 1)

    def emit_ffn(tag):
        arena()
        nmrow = sb(tag + "_nmr", [1, NTB, 512], F32)
        OH = sb(tag + "_oh", [8, T], BF16)
        SBT[tag + "OH"] = OH
        mark = AL[0]
        emit_ln(tag + "ln", lambda c, tb: XN[:, c, :] if tb is None
                else XN[:, c, 512 * tb:512 * tb + 512], negmean=nmrow, reset=False)
        if fcut < 0:
            return
        gw = sb(tag + "_gw", [128, 8, 8], F32)
        g1col = sb(tag + "_g1c", [8, 1], F32)
        g1row = sb(tag + "_g1r", [1, 8], F32)
        GS = sb(tag + "_gs", [8, 2, 512], F32)
        ST = sb(tag + "_stm", [128, 16, 8], F32)
        MX = sb(tag + "_mx", [128, 16], F32)
        CS = sb(tag + "_csb", [128, 3, 16, 8], F32)
        OHt = sb(tag + "_oht", [128, 16, 8], F32)

        with nc.Block() as blk:
            @blk.sync
            def _(sy):
                S.inc(sy.dma_start(gw[...], dr["gatewT"][...].rearrange(
                    "(c p) e -> p c e", p=128)), tag + "gwd", 16)

            @blk.tensor
            def _(t):
                S.wait(t, tag + "gwd", 16)
                for c in range(CH):
                    ins = t.matmul(pb[6][0:8, 0:1], gw[:, c, :], onesr_f[:, 0:1],
                                   start=(c == 0), stop=(c == CH - 1))
                S.inc(ins, tag + "g1a", 1)
                S.wait(t, tag + "g1b", 1)
                ins = t.matmul(pb[6][0:1, 0:8], g1col[...], identf[0:8, 0:8],
                               start=True, stop=True)
                S.inc(ins, tag + "g1c", 1)
                S.wait(t, tag + "g1d", 1)
                for tb in range(NTB):
                    S.wait(t, tag + "lnnm", tb + 1)
                    if tb >= 1:
                        S.wait(t, tag + "gsev", tb)
                    for c in range(CH):
                        t.matmul(pb[0][0:8, :], gw[:, c, :],
                                 X[:, c, 512 * tb:512 * tb + 512],
                                 start=(c == 0), stop=False)
                    ins = t.matmul(pb[0][0:8, :], g1row[...], nmrow[:, tb, :],
                                   start=False, stop=True)
                    S.inc(ins, tag + "gmm", 1)
                    S.wait(t, tag + "gsev", tb + 1)
                    for k in range(4):
                        ins = t.transpose(pb[1][:, 32 * tb + 8 * k:32 * tb + 8 * k + 8],
                                          GS[0:8, tb % 2, 128 * k:128 * k + 128],
                                          identf[0:8, 0:8])
                    S.inc(ins, tag + "gtp", 1)
                S.wait(t, tag + "ohm", 1)
                for i in range(16):
                    if i % 4 == 0 and i > 0:
                        S.wait(t, tag + "ohev", i // 4)
                    ins = t.transpose(pb[2][0:8, 128 * (i % 4):128 * (i % 4) + 128],
                                      OHt[:, i, :], identf[...])
                    if i % 4 == 3:
                        S.inc(ins, tag + "ohtp", 1)

            @blk.vector
            def _(v):
                S.wait(v, tag + "g1a", 1)
                S.inc(v.tensor_copy(g1col[...], pb[6][0:8, 0:1]), tag + "g1b", 1)
                S.wait(v, tag + "g1c", 1)
                S.inc(v.tensor_copy(g1row[...], pb[6][0:1, 0:8]), tag + "g1d", 1)
                for tb in range(NTB):
                    S.wait(v, tag + "gmm", tb + 1)
                    if tb >= 2:
                        S.wait(v, tag + "gtp", tb - 1)
                    S.inc(v.tensor_copy(GS[0:8, tb % 2, :], pb[0][0:8, :]),
                          tag + "gsev", 1)
                S.wait(v, tag + "gtp", NTB)
                v.tensor_copy(ST[...], pb[1][:, 0:128].rearrange(
                    "p (a b) -> p a b", a=16))
                v.reduce_max(MX[...], ST[...], axis=AX)
                v.tensor_copy(junkB[...], junkA[...])
                v.tensor_copy(junkB[...], junkA[...])
                v.tensor_tensor(CS[:, 0, :, :], ST[...],
                                MX[...].unsqueeze(2).broadcast_to([128, 16, 8]),
                                OP.is_ge)
                v.tensor_copy(CS[:, 1, :, 0:1], CS[:, 0, :, 0:1])
                v.tensor_add(CS[:, 1, :, 1:], CS[:, 0, :, 1:], CS[:, 0, :, :-1])
                v.tensor_copy(CS[:, 2, :, 0:2], CS[:, 1, :, 0:2])
                v.tensor_add(CS[:, 2, :, 2:], CS[:, 1, :, 2:], CS[:, 1, :, :-2])
                v.tensor_copy(CS[:, 1, :, 0:4], CS[:, 2, :, 0:4])
                v.tensor_add(CS[:, 1, :, 4:], CS[:, 2, :, 4:], CS[:, 2, :, :-4])
                v.tensor_single_scalar(CS[:, 2, :, :], CS[:, 1, :, :], 1.0, OP.is_equal)
                ins = v.tensor_mul(OHt[...], CS[:, 2, :, :], CS[:, 0, :, :])
                S.inc(ins, tag + "ohm", 1)
                for i4 in range(4):
                    S.wait(v, tag + "ohtp", i4 + 1)
                    ins = v.tensor_copy(OH[0:8, 512 * i4:512 * i4 + 512],
                                        pb[2][0:8, 0:512])
                    S.inc(ins, tag + "ohev", 1)

        AL[0] = mark
        if fcut < 1:
            return
        MBall = sb(tag + "_mba", [128, 8, T], mybir.dt.uint8)
        SBT[tag + "MB"] = MBall
        acc = sb(tag + "_acc", [128, 2, T], F32)
        WE = sb(tag + "_we", [128, 2, 8, 128], BF16)

        with nc.Block() as blk:
            @blk.tensor
            def _(t):
                for e in range(8):
                    for tb in range(NTB):
                        n = e * NTB + tb
                        if n >= 2:
                            S.wait(t, tag + "mbev", n - 1)
                        ins = t.matmul(pb[n % 2][:, :], selcol8[:, e, :],
                                       OH[0:8, 512 * tb:512 * tb + 512],
                                       start=True, stop=True)
                        S.inc(ins, tag + "mbc", 1)

            @blk.vector
            def _(v):
                for e in range(8):
                    for tb in range(NTB):
                        n = e * NTB + tb
                        S.wait(v, tag + "mbc", n + 1)
                        ins = v.tensor_copy(MBall[:, e, 512 * tb:512 * tb + 512],
                                            pb[n % 2][:, :])
                        S.inc(ins, tag + "mbev", 1)

        if fcut < 2:
            return
        with nc.Block() as blk:
            @blk.gpsimd
            def _(g_):
                for i in range(CH):
                    for e in range(8):
                        sl = i * 8 + e
                        if sl >= 2:
                            S.wait(g_, tag + "femm", sl - 1)
                        S.inc(g_.dma_start(
                            WE[:, sl % 2, :, :],
                            dr["expwT"][e, :, 128 * i:128 * i + 128].rearrange(
                                "(c p) j -> p c j", p=128)),
                            tag + "wed%d" % (sl % 2), 16)

            @blk.tensor
            def _(t):
                for i in range(CH):
                    for e in range(8):
                        sl = i * 8 + e
                        S.wait(t, tag + "wed%d" % (sl % 2), 16 * (sl // 2 + 1))
                        if sl >= 2:
                            S.wait(t, tag + "fsel", 4 * (sl - 1))
                        bk = 4 * (e % 2)
                        for j in range(-1, CH):
                            stat = identb[...] if j < 0 else WE[:, sl % 2, j, :]
                            src = XN[:, i if j < 0 else j, :]
                            for tb in range(NTB):
                                ins = t.matmul(pb[bk + tb][:, :], stat,
                                               src[:, 512 * tb:512 * tb + 512],
                                               start=(j == -1), stop=(j == CH - 1))
                        S.inc(ins, tag + "femm", 1)

            @blk.vector
            def _(v):
                v.memset(acc[...], 0.0)
                for i in range(CH):
                    for e in range(8):
                        sl = i * 8 + e
                        S.wait(v, tag + "femm", sl + 1)
                        bk = 4 * (e % 2)
                        cur, prv = e % 2, 1 - e % 2
                        for tb in range(NTB):
                            ins = v.select(acc[:, cur, 512 * tb:512 * tb + 512],
                                           MBall[:, e, 512 * tb:512 * tb + 512],
                                           pb[bk + tb][:, :],
                                           acc[:, prv, 512 * tb:512 * tb + 512])
                            S.inc(ins, tag + "fsel", 1)
                        v.tensor_copy(junkB[...], junkA[...])
                        v.tensor_copy(junkB[...], junkA[...])
                    v.tensor_copy(junkB[...], junkA[...])
                    v.tensor_copy(junkB[...], junkA[...])
                    ins = v.scalar_tensor_tensor(X[:, i, :], acc[:, 1, :], 1.0,
                                                 X[:, i, :], OP.mult, OP.add)

    if stage >= 5:
        emit_ffn("f1")

    H_AT = {(0, 0): 0, (1, 0): 1, (2, 0): 4, (3, 0): 5,
            (0, 1): 2, (1, 1): 3, (2, 1): 6, (3, 1): 7}
    if stage >= 6:
        emit_ln("ln3", lambda c, tb: XN[:, c, :] if tb is None
                else XN[:, c, 512 * tb:512 * tb + 512])
        arena()
        QB2 = sb("b_q", [128, 4, TEP], BF16)
        KB2 = sb("b_k", [128, 2, TEP], BF16)
        VTB = sb("b_vt", [128, 2, KTB, 194], BF16)
        SX = sb("b_sx", [128, 4, T], BF16)
        MT = sb("b_mt", [128, 8, 4], BF16)
        MARK6 = AL[0]
        mld = sb("b_mld", [4, D], F32)
        AL[0] = MARK6
        wqb = sb("b_wq", [128, 2, 8, 128], BF16)
        vraw = nc.alloc_sbuf_tensor_at("b_vraw", [128, 1, TEP], F32, offset=114688)
        _xb = 16384 + 64 * 1024
        sqb = nc.alloc_sbuf_tensor_at("b_sqb", [128, 512], BF16, offset=_xb)
        qnf = nc.alloc_sbuf_tensor_at("b_qnf", [128, 512], F32, offset=_xb + 1024)
        t1f = nc.alloc_sbuf_tensor_at("b_t1f", [128, 512], F32, offset=_xb + 3072)
        t2f = nc.alloc_sbuf_tensor_at("b_t2f", [128, 512], F32, offset=_xb + 5120)
        rws = nc.alloc_sbuf_tensor_at("b_rws", [2, 3, 512], F32, offset=_xb + 7168)
        rotc = nc.alloc_sbuf_tensor_at("b_rotc", [128, TEP], F32,
                                       offset=_xb + 13312)
        rots = nc.alloc_sbuf_tensor_at("b_rots", [128, TEP], F32,
                                       offset=_xb + 13312 + TEP * 4)
        SBT["bQ"] = QB2
        SBT["bK"] = KB2
        SBT["bSX"] = SX

        with nc.Block() as blk:
            @blk.gpsimd
            def _(g_):
                S.inc(g_.dma_start(mld[...], dr["meta"][...]), "bmld", 16)

            @blk.tensor
            def _(t):
                S.wait(t, "bmld", 16)
                for c in range(CH):
                    ins = t.transpose(pb[7][:, 4 * c:4 * c + 4],
                                      mld[0:4, 128 * c:128 * c + 128],
                                      identf[0:4, 0:4])
                S.inc(ins, "bmtp", 1)

            @blk.vector
            def _(v):
                S.wait(v, "bmtp", 1)
                v.tensor_copy(MT[...], pb[7][:, 0:32].rearrange(
                    "p (c q) -> p c q", c=8))
                v.memset(QB2[:, :, TE:], 0.0)
                v.memset(KB2[:, :, TE:], 0.0)
                v.memset(vraw[:, 0, TE:], 0.0)
                v.memset(VTB[:, :, :, 64:66], 1.0)
                v.memset(VTB[:, :, :, 66:129], 0.0)

        PLAN = ([("cqwT", 128 * c, "q", c) for c in range(4)]
                + [("ckwT", 128 * c, "k", c) for c in range(2)]
                + [("cvwT", 128 * c, "v", c) for c in range(2)]
                + [("ssmwT", 128 * c, "s", c) for c in range(4)])
        NG = [5 if k in ("q", "k", "v") else 4 for (_, _, k, _) in PLAN]
        CUM = [0]
        for x_ in NG:
            CUM.append(CUM[-1] + x_)

        with nc.Block() as blk:
            @blk.gpsimd
            def _(g_):
                for n, (wnm, col0, kind, c) in enumerate(PLAN):
                    if n >= 2:
                        S.wait(g_, "bpmm", CUM[n - 1])
                    S.inc(g_.dma_start(
                        wqb[:, n % 2, :, :],
                        dr[wnm][:, col0:col0 + 128].rearrange(
                            "(c p) j -> p c j", p=128)), "bwq%d" % (n % 2), 16)

            @blk.tensor
            def _(t):
                gidx = 0
                vti = 0
                for n, (wnm, col0, kind, c) in enumerate(PLAN):
                    S.wait(t, "bwq%d" % (n % 2), 16 * (n // 2 + 1))
                    for tt in range(4):
                        if gidx >= 2:
                            S.wait(t, "bpev", gidx - 1)
                        bank = pb[gidx % 2]
                        for cc in range(CH):
                            ins = t.matmul(bank[:, 0:512], wqb[:, n % 2, cc, :],
                                           XN[:, cc, 512 * tt:512 * tt + 512],
                                           start=(cc == 0), stop=(cc == CH - 1))
                        S.inc(ins, "bpmm", 1)
                        gidx += 1
                    if kind in ("q", "k", "v"):
                        if gidx >= 2:
                            S.wait(t, "bpev", gidx - 1)
                        bank = pb[gidx % 2]
                        for cc in range(CH):
                            ins = t.matmul(bank[:, 0:4], wqb[:, n % 2, cc, :],
                                           MT[:, cc, :], start=(cc == 0),
                                           stop=(cc == CH - 1))
                        S.inc(ins, "bpmm", 1)
                        gidx += 1
                    if kind == "v":
                        S.wait(t, "bpev", CUM[n + 1])
                        for kt in range(KTB):
                            if vti >= 2:
                                S.wait(t, "bvtev", vti - 1)
                            ins = t.transpose(pb[5 + vti % 2][:, 0:128],
                                              vraw[:, 0, 128 * kt:128 * kt + 128],
                                              identf[...])
                            S.inc(ins, "bvtp", 1)
                            vti += 1

            @blk.vector
            def _(v):
                gidx = 0
                vti = 0
                for n, (wnm, col0, kind, c) in enumerate(PLAN):
                    dst = {"q": QB2, "k": KB2}.get(kind)
                    for tt in range(4):
                        S.wait(v, "bpmm", gidx + 1)
                        bank = pb[gidx % 2]
                        if kind == "s":
                            ins = v.tensor_copy(SX[:, c, 512 * tt:512 * tt + 512],
                                                bank[:, 0:512])
                        elif kind == "v":
                            if c == 1 and tt == 0:
                                S.wait(v, "bvtp", KTB)
                            ins = v.tensor_copy(
                                vraw[:, 0, 4 + 512 * tt:516 + 512 * tt],
                                bank[:, 0:512])
                        else:
                            ins = v.tensor_copy(
                                dst[:, c, 4 + 512 * tt:516 + 512 * tt],
                                bank[:, 0:512])
                        S.inc(ins, "bpev", 1)
                        gidx += 1
                    if kind in ("q", "k", "v"):
                        S.wait(v, "bpmm", gidx + 1)
                        bank = pb[gidx % 2]
                        if kind == "v":
                            ins = v.tensor_copy(vraw[:, 0, 0:4], bank[:, 0:4])
                        else:
                            ins = v.tensor_copy(dst[:, c, 0:4], bank[:, 0:4])
                        S.inc(ins, "bpev", 1)
                        gidx += 1
                    if kind == "v":
                        for kt in range(KTB):
                            S.wait(v, "bvtp", vti + 1)
                            bview = pb[5 + vti % 2][:, 0:128].rearrange(
                                "p (h q) -> p h q", h=2)
                            v.tensor_copy(VTB[:, c, kt, 0:64], bview[:, 0, :])
                            ins = v.tensor_copy(VTB[:, c, kt, 129:193],
                                                bview[:, 1, :])
                            S.inc(ins, "bvtev", 1)
                            vti += 1

        RCH = [(QB2, c) for c in range(4)] + [(KB2, c) for c in range(2)]
        NT6 = 5

        with nc.Block() as blk:
            @blk.gpsimd
            def _(g_):
                S.inc(g_.dma_start(rotc[...], dr["rotc"][...]), "brld", 16)
                S.inc(g_.dma_start(rots[...], dr["rots"][...]), "brld", 16)

            @blk.vector
            def _(v):
                for rn, (buf, c) in enumerate(RCH):
                    for tt in range(NT6):
                        w = 512 if tt < 4 else 128
                        ri = rn * NT6 + tt
                        cs = slice(512 * tt, 512 * tt + w)
                        IN = buf[:, c, cs]
                        if ri >= 1:
                            S.wait(v, "brnm", ri)
                        S.inc(v.tensor_mul(sqb[:, 0:w], IN, IN), "brsq", 1)
                        S.wait(v, "brnm", ri + 1)
                        if ri >= 1:
                            S.wait(v, "brst", ri)
                        S.inc(v.tensor_copy(rws[:, 0, 0:w], pb[3][0:2, 0:w]),
                              "brnmev", 1)
                        S.wait(v, "brbc", ri + 1)
                        S.inc(v.tensor_tensor(qnf[:, 0:w], IN, pb[4][:, 0:w],
                                              OP.mult), "brqn", 1)
                        if ri == 0:
                            S.wait(v, "brld", 32)
                        S.wait(v, "brsw", ri + 1)
                        v.tensor_mul(t1f[:, 0:w], qnf[:, 0:w], rotc[:, cs])
                        v.tensor_tensor(t2f[:, 0:w], rots[:, cs], pb[5][:, 0:w],
                                        OP.mult)
                        ins = v.tensor_add(IN, t1f[:, 0:w], t2f[:, 0:w])
                        S.inc(ins, "brot", 1)

            @blk.tensor
            def _(t):
                for rn, (buf, c) in enumerate(RCH):
                    for tt in range(NT6):
                        w = 512 if tt < 4 else 128
                        ri = rn * NT6 + tt
                        S.wait(t, "brsq", ri + 1)
                        if ri >= 1:
                            S.wait(t, "brnmev", ri)
                        ins = t.matmul(pb[3][0:2, 0:w], bd2[...], sqb[:, 0:w],
                                       start=True, stop=True)
                        S.inc(ins, "brnm", 1)
                        S.wait(t, "brst", ri + 1)
                        if ri >= 1:
                            S.wait(t, "brqn", ri)
                        ins = t.matmul(pb[4][:, 0:w], bd2t[...], rws[0:2, 2, 0:w],
                                       start=True, stop=True)
                        S.inc(ins, "brbc", 1)
                        S.wait(t, "brqn", ri + 1)
                        if ri >= 1:
                            S.wait(t, "brot", ri)
                        ins = t.matmul(pb[5][:, 0:w], swapperm[...], qnf[:, 0:w],
                                       start=True, stop=True)
                        S.inc(ins, "brsw", 1)

            @blk.scalar
            def _(s):
                for ri in range(6 * NT6):
                    w = 512 if ri % NT6 < 4 else 128
                    S.wait(s, "brnmev", ri + 1)
                    if ri >= 1:
                        S.wait(s, "brbc", ri)
                    s.activation(rws[:, 1, 0:w], rws[:, 0, 0:w], AF.Ln,
                                 scale=1.0 / 64, bias=1e-6)
                    ins = s.activation(rws[:, 2, 0:w], rws[:, 1, 0:w], AF.Exp,
                                       scale=-0.5)
                    S.inc(ins, "brst", 1)

    if stage >= 7:
        XNOFF = [None]

        def sbx(name, shape, dtype):
            per = int(np.prod(shape[1:])) * mybir.dt.size(dtype)
            off = XNOFF[0]
            t_ = nc.alloc_sbuf_tensor_at(name, shape, dtype, offset=off)
            XNOFF[0] = off + per
            return t_

        import math as _m
        XNOFF[0] = 16384
        xn_base = 16384 + 64 * 1024
        XNOFF[0] = xn_base
        S0 = sbx("s_0", [128, 1024], F32)
        S1 = sbx("s_1", [128, 1024], F32)
        S2 = sbx("s_2", [128, 1024], F32)
        S3 = sbx("s_3", [128, 1024], F32)
        S4 = sbx("s_4", [128, 1024], F32)
        S5 = sbx("s_5", [128, 1024], F32)
        S6 = sbx("s_6", [128, 1024], F32)
        HB = sbx("s_hb", [128, 1024], BF16)
        CHB = sbx("s_chb", [128, 1024], BF16)
        assert XNOFF[0] <= xn_base + 32 * 1024 + 1, XNOFF[0]
        AL[0] = MARK6
        H0B = sb("s_h0", [128, 4], F32)
        YRW = sb("s_yr", [8, 2, 512], F32)
        YXR = sb("s_yx", [8, 2, 512], F32)

        with nc.Block() as blk:
            @blk.tensor
            def _(t):
                for hf in range(2):
                    C0 = 1024 * hf
                    for tb2 in range(2):
                        cs = slice(C0 + 512 * tb2, C0 + 512 * tb2 + 512)
                        n7 = hf * 2 + tb2
                        for qi, (zz, _sem) in enumerate(
                                ((zdt, "sdt"), (zbx, "sbx"), (zcc, "scc"))):
                            if n7 >= 1:
                                S.wait(t, "sexd" if qi == 0 else "sev%d" % qi, n7)
                            for cc in range(4):
                                ins = t.matmul(
                                    pb[qi][:, :],
                                    zz[:, cc, :],
                                    SX[:, cc, cs],
                                    start=(cc == 0), stop=(cc == 3))
                            S.inc(ins, "smm%d" % qi, 1)
                    for tb2 in range(2):
                        cs = slice(C0 + 512 * tb2, C0 + 512 * tb2 + 512)
                        hs = slice(512 * tb2, 512 * tb2 + 512)
                        n7 = hf * 2 + tb2
                        S.wait(t, "schb", hf + 1)
                        if n7 >= 1:
                            S.wait(t, "syev", n7)
                        ins = t.matmul(pb[3][0:8, :], bd8[...], CHB[:, hs],
                                       start=True, stop=True)
                        S.inc(ins, "symm", 1)
                        if n7 >= 1:
                            S.wait(t, "syx", n7)
                        for cc in range(4):
                            ins = t.matmul(pb[4][0:8, :], selx0[:, cc, :],
                                           SX[:, cc, cs], start=(cc == 0),
                                           stop=(cc == 3))
                        S.inc(ins, "sxmm", 1)
                        for c in range(4):
                            oi = n7 * 4 + c
                            S.wait(t, "syx", n7 + 1)
                            if oi >= 2:
                                S.wait(t, "soev", oi - 1)
                            ins = t.matmul(pb[5 + c % 2][:, :],
                                           bdow[:, c, :],
                                           HB[:, hs],
                                           start=True, stop=False)
                            ins = t.matmul(pb[5 + c % 2][:, :],
                                           e8c[:, c, :],
                                           YXR[0:8, tb2, :],
                                           start=False, stop=True)
                            S.inc(ins, "somm", 1)

            @blk.scalar
            def _(s):
                for hf in range(2):
                    for tb2 in range(2):
                        n7 = hf * 2 + tb2
                        hs = slice(512 * tb2, 512 * tb2 + 512)
                        S.wait(s, "smm0", n7 + 1)
                        if hf >= 1:
                            S.wait(s, "su", hf)
                        ins = s.activation(S0[:, hs], pb[0][:, :], AF.Exp,
                                           bias=dtb[...])
                        S.inc(ins, "sexd", 1)
                        S.inc(s.activation(S3[:, hs], S0[:, hs], AF.Ln,
                                           bias=1.0), "ssp", 1)
                    S.wait(s, "sclip", hf + 1)
                    s.activation(S6[...], S4[...], AF.Exp)
                    S.inc(s.activation(S0[...], S4[...], AF.Exp, scale=-1.0),
                          "spexp", 1)
                for hf in range(2):
                    pass

            @blk.vector
            def _(v):
                for hf in range(2):
                    C0 = 1024 * hf
                    for tb2 in range(2):
                        n7 = hf * 2 + tb2
                        hs = slice(512 * tb2, 512 * tb2 + 512)
                        S.wait(v, "smm1", n7 + 1)
                        S.inc(v.tensor_copy(S1[:, hs], pb[1][:, :]), "sev1", 1)
                        S.wait(v, "smm2", n7 + 1)
                        S.inc(v.tensor_copy(S2[:, hs], pb[2][:, :]), "sev2", 1)
                    S.wait(v, "ssp", 2 * hf + 2)
                    v.tensor_scalar_min(S0[...], S3[...], 1.0)
                    v.tensor_scalar(S4[...], S0[...], negA[...], -0.5,
                                    OP.mult, OP.max)
                    v.tensor_mul(S5[...], S1[...], S0[...])
                    for ch in range(16):
                        v.tensor_tensor_scan(S6[:, 64 * ch:64 * ch + 64],
                                             S4[:, 64 * ch:64 * ch + 64],
                                             junkA[...], 0.0, OP.add, OP.add)
                    ins = v.tensor_scalar(S4[...], S6[...], -30.0, 0.0,
                                          OP.max, OP.min)
                    S.inc(ins, "sclip", 1)
                    S.wait(v, "spexp", hf + 1)
                    S.inc(v.tensor_mul(S1[...], S5[...], S0[...]), "su", 1)
                    for ch in range(16):
                        v.tensor_tensor_scan(S5[:, 64 * ch:64 * ch + 64],
                                             S1[:, 64 * ch:64 * ch + 64],
                                             junkA[...], 0.0, OP.add, OP.add)
                    v.tensor_mul(S1[...], S6[...], S5[...])
                    for ch in range(16):
                        if hf == 0 and ch == 0:
                            continue
                        if ch == 0:
                            h0 = H0B[:, 0:1]
                        else:
                            h0 = S1[:, 64 * ch - 1:64 * ch]
                        v.tensor_copy(junkB[...], junkA[...])
                        v.tensor_copy(junkB[...], junkA[...])
                        v.scalar_tensor_tensor(S1[:, 64 * ch:64 * ch + 64],
                                               S6[:, 64 * ch:64 * ch + 64], h0,
                                               S1[:, 64 * ch:64 * ch + 64],
                                               OP.mult, OP.add)
                    v.tensor_copy(junkB[...], junkA[...])
                    v.tensor_copy(junkB[...], junkA[...])
                    if hf == 0:
                        v.tensor_copy(H0B[:, 0:1], S1[:, 1023:1024])
                    v.tensor_copy(HB[...], S1[...])
                    ins = v.tensor_mul(CHB[...], S2[...], S1[...])
                    S.inc(ins, "schb", 1)
                    for tb2 in range(2):
                        n7 = hf * 2 + tb2
                        cs = slice(C0 + 512 * tb2, C0 + 512 * tb2 + 512)
                        S.wait(v, "symm", n7 + 1)
                        S.inc(v.tensor_copy(YRW[:, tb2, :], pb[3][0:8, :]),
                              "syev", 1)
                        S.wait(v, "sxmm", n7 + 1)
                        ins = v.tensor_tensor(YXR[:, tb2, :], YRW[:, tb2, :],
                                              pb[4][0:8, :], OP.mult)
                        S.inc(ins, "syx", 1)
                        for c in range(4):
                            oi = n7 * 4 + c
                            S.wait(v, "somm", oi + 1)
                            ins = v.tensor_scalar_mul(
                                AO[:, 4 + c, cs], pb[5 + c % 2][:, :],
                                sscale_b[...])
                            S.inc(ins, "soev", 1)
    if stage >= 7 and not skip_p7b:
        stb7 = nc.alloc_sbuf_tensor_at("b_stb", [128, 4, 512], BF16,
                                       offset=xn_base)
        anr7 = nc.alloc_sbuf_tensor_at("b_anr", [128, 2, 512], F32,
                                       offset=xn_base + 4096)
        rcb7 = nc.alloc_sbuf_tensor_at("b_rcb", [128, 512], F32,
                                       offset=xn_base + 8192)
        CAO = {}
        for cc_ in range(4):
            for bb_ in range(2):
                CAO[H_AT[(cc_, bb_)]] = cc_

        SCHED = []
        for h in range(8):
            for qb in range(5):
                SCHED.append((h, qb, list(range(min(KTB, 4 * qb + 4)))))
        AVCUM = [0]
        for (_h, _qb, _kts) in SCHED:
            AVCUM.append(AVCUM[-1] + len(_kts))

        with nc.Block() as blk:
            @blk.tensor
            def _(t):
                ei = 0
                mi = 0
                for un, (h, qb, kts) in enumerate(SCHED):
                    kvh = h // 2
                    kc, ksub = kvh // 2, kvh % 2
                    qc, qsub = 2 * (h // 4) + (h % 2), ksub
                    qw = 512 if qb < 4 else 128
                    q0 = 512 * qb
                    for kt in kts:
                        strad = kt >= 4 * qb
                        if ei >= 2:
                            S.wait(t, "baexp", ei - 1)
                        ins = t.matmul(pb[3 + ei % 2][:, 0:qw],
                                       KB2[64 * ksub:64 * ksub + 64, kc,
                                           128 * kt:128 * kt + 128],
                                       QB2[64 * qsub:64 * qsub + 64, qc,
                                           q0:q0 + qw],
                                       start=True, stop=True)
                        S.inc(ins, "bascr", 1)
                        if strad:
                            mi += 1
                            S.wait(t, "bamsk", mi)
                            slot = 2 + ei % 2
                        else:
                            S.wait(t, "baexp", ei + 1)
                            slot = ei % 2
                        if kt == kts[0] and un >= 2:
                            S.wait(t, "banorm", un - 1)
                        lhs = (VTB[:, kc, kt, 0:65] if ksub == 0
                               else VTB[:, kc, kt, 65:193])
                        avo = (pb[5 + un % 2][0:65, 0:qw] if ksub == 0
                               else pb[5 + un % 2][:, 0:qw])
                        ins = t.matmul(avo, lhs, stb7[:, slot, 0:qw],
                                       start=(kt == kts[0]),
                                       stop=(kt == kts[-1]))
                        S.inc(ins, "baavk", 1)
                        ei += 1
                    S.wait(t, "barc", un + 1)
                    if un >= 1:
                        S.wait(t, "banorm", un)
                    srp = 64 if ksub == 0 else 0
                    ins = t.matmul(pb[7][:, 0:qw], onesr_f[srp:srp + 1, 0:128],
                                   anr7[srp:srp + 1, 1, 0:qw],
                                   start=True, stop=True)
                    S.inc(ins, "barcbc", 1)

            @blk.scalar
            def _(s):
                ei = 0
                for un, (h, qb, kts) in enumerate(SCHED):
                    kvh = h // 2
                    ksub = kvh % 2
                    qw = 512 if qb < 4 else 128
                    for kt in kts:
                        S.wait(s, "bascr", ei + 1)
                        if ei >= 2:
                            S.wait(s, "baavk", ei - 1)
                        ins = s.activation(stb7[:, ei % 2, 0:qw],
                                           pb[3 + ei % 2][:, 0:qw], AF.Exp,
                                           bias=bb_b[:, h:h + 1],
                                           scale=gs_b[:, h:h + 1])
                        S.inc(ins, "baexp", 1)
                        ei += 1
                    S.wait(s, "baavk", AVCUM[un + 1])
                    srp = 64 if ksub == 0 else 0
                    av = pb[5 + un % 2]
                    sr = 64 if ksub == 0 else 0
                    s.activation(anr7[sr:sr + 1, 0, 0:qw], av[sr:sr + 1, 0:qw],
                                 AF.Ln)
                    S.inc(s.activation(anr7[sr:sr + 1, 1, 0:qw],
                                       anr7[sr:sr + 1, 0, 0:qw], AF.Exp,
                                       scale=-1.0), "barc", 1)

            @blk.vector
            def _(v):
                ei = 0
                mi = 0
                for un, (h, qb, kts) in enumerate(SCHED):
                    kvh = h // 2
                    ksub = kvh % 2
                    band = ksub
                    c_ao = CAO[h]
                    qw = 512 if qb < 4 else 128
                    q0 = 512 * qb
                    for kt in kts:
                        strad = kt >= 4 * qb
                        if strad:
                            S.wait(v, "baexp", ei + 1)
                            if mi >= 2:
                                S.wait(v, "baavk", ei - 1)
                            off = 128 * (kt - 4 * qb)
                            mw = min(128, qw - off)
                            slot = 2 + ei % 2
                            if off > 0:
                                v.memset(stb7[:, slot, 0:off], 0.0)
                            if off + mw < qw:
                                v.tensor_copy(stb7[:, slot, off + mw:qw],
                                              stb7[:, ei % 2, off + mw:qw])
                            ins = v.tensor_mul(stb7[:, slot, off:off + mw],
                                               trilb[:, 0:mw],
                                               stb7[:, ei % 2, off:off + mw])
                            S.inc(ins, "bamsk", 1)
                            mi += 1
                        ei += 1
                    S.wait(v, "barcbc", un + 1)
                    v.tensor_copy(rcb7[:, 0:qw], pb[7][:, 0:qw])
                    av = pb[5 + un % 2]
                    src = av[64 * band:64 * band + 64, 0:qw] if False else (
                        av[0:64, 0:qw] if ksub == 0 else av[64:128, 0:qw])
                    if qb == 0:
                        ss, d0, d1 = slice(4, 512), 0, 508
                    elif qb < 4:
                        ss, d0, d1 = slice(0, 512), q0 - 4, q0 + 508
                    else:
                        ss, d0, d1 = slice(0, 4), 2044, 2048
                    dst = AO[64 * band:64 * band + 64, c_ao, d0:d1]
                    ins = v.scalar_tensor_tensor(
                        dst, src[:, ss], ascale_b[64 * band:64 * band + 64, 0:1],
                        rcb7[64 * band:64 * band + 64, ss], OP.mult, OP.mult)
                    S.inc(ins, "banorm", 1)
    if stage >= 8:
        arena()
        wpb = sb("pb_w", [128, 2, 8, 128], BF16)
        with nc.Block() as blk:
            @blk.gpsimd
            def _(g_):
                for i in range(CH):
                    if i >= 2:
                        S.wait(g_, "pbmm", 4 * (i - 1))
                    S.inc(g_.dma_start(
                        wpb[:, i % 2, :, :],
                        dr["projbT"][:, 128 * i:128 * i + 128].rearrange(
                            "(c p) j -> p c j", p=128)), "pbw%d" % (i % 2), 16)

            @blk.tensor
            def _(t):
                for i in range(CH):
                    S.wait(t, "pbw%d" % (i % 2), 16 * (i // 2 + 1))
                    for tb in range(NTB):
                        un = i * NTB + tb
                        if un >= 2:
                            S.wait(t, "pbev", un - 1)
                        for j in range(CH):
                            ins = t.matmul(pb[un % 2][:, :], wpb[:, i % 2, j, :],
                                           AO[:, j, 512 * tb:512 * tb + 512],
                                           start=(j == 0), stop=(j == CH - 1))
                        S.inc(ins, "pbmm", 1)

            @blk.vector
            def _(v):
                for i in range(CH):
                    for tb in range(NTB):
                        un = i * NTB + tb
                        S.wait(v, "pbmm", un + 1)
                        dst = X[:, i, 512 * tb:512 * tb + 512]
                        ins = v.scalar_tensor_tensor(dst, pb[un % 2][:, :], 1.0,
                                                     dst, OP.mult, OP.add)
                        S.inc(ins, "pbev", 1)

    if stage >= 9:
        emit_ffn("f2")

    if taps:
        with nc.Block() as blk:
            @blk.gpsimd
            def _(g):
                for nm in taps:
                    t_ = SBT[nm]
                    td = nc.dram_tensor("tap_" + nm, list(t_.shape), F32,
                                        kind="ExternalOutput")
                    S.inc(g.dma_start(td[...], t_[...]), "tdma", 16)
                S.wait(g, "tdma", S.tot("tdma"))

    arena()
    ot = sb("ot", [128, 2, 1024], F32)
    with nc.Block() as blk:
        @blk.tensor
        def _(t):
            for i in range(16):
                if i >= 2:
                    S.wait(t, "oev", 2 * (i - 2) + 2)
                for c in range(CH):
                    bank = pb[(i % 2) * 2 + c // 4]
                    ins = t.transpose(bank[:, 128 * (c % 4):128 * (c % 4) + 128],
                                      X[:, c, 128 * i:128 * i + 128], identf[...])
                    if c % 4 == 3:
                        S.inc(ins, "otp", 1)

        @blk.vector
        def _(v):
            for i in range(16):
                if i >= 2:
                    S.wait(v, "odma%d" % (i % 2), 16 * (i // 2))
                for b in range(2):
                    S.wait(v, "otp", 2 * i + b + 1)
                    bank = pb[(i % 2) * 2 + b]
                    ins = v.tensor_copy(ot[:, i % 2, 512 * b:512 * b + 512], bank[...])
                    S.inc(ins, "oev", 1)

        @blk.sync
        def _(sy):
            for i in range(16):
                S.wait(sy, "oev", 2 * i + 2)
                S.inc(sy.dma_start(out_d[128 * i:128 * i + 128, :], ot[:, i % 2, :]),
                      "odma%d" % (i % 2), 16)
            S.wait(sy, "odma0", S.tot("odma0"))
            S.wait(sy, "odma1", S.tot("odma1"))

    assert AL[0] <= 229344 - 32, AL[0]
    return nc


def kernel(**inputs):
    consts = _host_prep(inputs)
    x = np.ascontiguousarray(np.asarray(inputs["L0_attn_a_x"], np.float32))
    nc = build(_STAGE, _TAPS)
    in_maps = [dict(consts, x=x[i]) for i in range(8)]
    res = run_bass_kernel_spmd(nc, in_maps, core_ids=list(range(8)))
    return np.stack([res.results[i]["out"] for i in range(8)], 0)
